# version 09
# baseline (speedup 1.0000x reference)
import contextlib
import numpy as np
import ml_dtypes
import concourse.bass as bass
import concourse.mybir as mybir
from concourse.bass_utils import run_bass_kernel_spmd

F32 = mybir.dt.float32
BF16 = mybir.dt.bfloat16
AF = mybir.ActivationFunctionType
ALU = mybir.AluOpType
AXX = mybir.AxisListType.X
BFNP = ml_dtypes.bfloat16

H = 16
DKH = 64
D = 1024
T = 2048
C = 64
NCH = T // C
W = 512
NWIN = T // W
CPW = W // C
IFF = 2048
NEXP = 8
SCALE = float(1.0 / np.sqrt(np.float32(DKH)))
EPS = 1e-5
NRING = 4


class Rec:

    class Inst:
        def __init__(self, rec):
            self.rec = rec

        def then_inc(self, sem, val=1):
            self.rec.ops.append(("__then_inc", (sem, val), {}))
            return self

    def __init__(self, name):
        self.__dict__["name"] = name
        self.__dict__["ops"] = []
        self.__dict__["fsem"] = None
        self.__dict__["fcount"] = 0

    def set_fence(self, sem):
        self.__dict__["fsem"] = sem

    @property
    def fc(self):
        return self.__dict__["fcount"]

    def inc_last(self, sem, val=1):
        self.__dict__["ops"].append(("__then_inc", (sem, val), {}))

    def __getattr__(self, meth):
        d = self.__dict__
        ops = d["ops"]

        def call(*a, **k):
            real = meth not in ("wait_ge", "nop")
            if d["fsem"] is not None and real:
                if d["fcount"] > 0:
                    ops.append(("wait_ge", (d["fsem"], d["fcount"]), {}))
                ops.append((meth, a, k))
                ops.append(("__then_inc", (d["fsem"], 1), {}))
                d["fcount"] += 1
            else:
                ops.append((meth, a, k))
            return Rec.Inst(self)

        return call

    def replay(self, eng):
        last = None
        for meth, a, k in self.__dict__["ops"]:
            if meth == "__then_inc":
                last.then_inc(*a)
            else:
                last = getattr(eng, meth)(*a, **k)


class Ctr:
    def __init__(self, handle):
        self.h = handle
        self.v = 0

    def inc(self, n=1):
        self.v += n
        return self.v


def build_program():
    nc = bass.Bass("TRN2", target_bir_lowering=False)

    def din(name, shape, dtype=F32):
        return nc.dram_tensor(name, shape, dtype, kind="ExternalInput")

    XT = din("XT", [D, T])
    XPT = din("XPT", [D, T])
    SEL = din("SEL", [128, 1])
    WQT = din("WQT", [D, D])
    WKT = din("WKT", [D, D])
    WVT = din("WVT", [D, D])
    WOT = din("WOT", [D, D])
    WGAT = din("WGAT", [D, H])
    BG = din("BG", [H, 1])
    WRT = din("WRT", [D, NEXP])
    RB = din("RB", [1, NEXP])
    WGS = din("WGS", [D, IFF])
    WUS = din("WUS", [D, IFF])
    WDS = din("WDS", [IFF, D])
    WGE = din("WGE", [NEXP, D, IFF])
    WUE = din("WUE", [NEXP, D, IFF])
    WDE = din("WDE", [NEXP, IFF, D])
    MASK = din("MASK", [128, C])
    IDF = din("IDF", [128, 128])
    ONESB = din("ONESB", [128, 128], BF16)
    ONE1B = din("ONE1B", [1, 128], BF16)
    ONE1F = din("ONE1F", [1, 128])
    SELPB = din("SELPB", [H, 8 * 128], BF16)
    SELPF = din("SELPF", [H, 8 * 128])
    OUT = nc.dram_tensor("OUT", [D, T], F32, kind="ExternalOutput")

    sy, gp, te, ve, sc = (Rec(n) for n in ("sync", "gpsimd", "tensor", "vector", "scalar"))

    stack = contextlib.ExitStack()
    with (
        nc.semaphore("sW") as _sW,
        nc.semaphore("sWA") as _sWA,
        nc.semaphore("sWB") as _sWB,
        nc.semaphore("sWC") as _sWC,
        nc.semaphore("sWD") as _sWD,
        nc.semaphore("sX") as _sX,
        nc.semaphore("sT") as _sT,
        nc.semaphore("sV") as _sV,
        nc.semaphore("sS") as _sS,
        nc.semaphore("sO") as _sO,
        nc.Block() as block,
    ):
        sW, sWA, sWB, sWC, sWD, sX, sT, sO = (
            Ctr(x) for x in (_sW, _sWA, _sWB, _sWC, _sWD, _sX, _sT, _sO))
        ring_sems = [(_sWA, sWA), (_sWB, sWB), (_sWC, sWC), (_sWD, sWD)]
        ve.set_fence(_sV)
        sc.set_fence(_sS)

        class FC:
            def __init__(self, rec):
                self.rec = rec

            def inc(self, n=1):
                return self.rec.fc

            @property
            def v(self):
                return self.rec.fc

        sV = FC(ve)
        sS = FC(sc)

        def AL(name, shape, dtype=F32):
            return nc.alloc_sbuf_tensor(name, shape, dtype)

        def ALS(name, shape, dtype=F32):
            return stack.enter_context(nc.sbuf_tensor(name, shape, dtype))

        R = AL("R", [128, 8 * T])
        XB = AL("XB", [128, 8 * T], BF16)
        sel = AL("sel", [128, 1])
        mask = AL("mask", [128, C])
        idf = AL("idf", [128, 128])
        onesb = AL("onesb", [128, 128], BF16)
        one1b = AL("one1b", [1, 128], BF16)
        one1f = AL("one1f", [1, 128])
        selpb = AL("selpb", [H, 8 * 128], BF16)
        selpf = AL("selpf", [H, 8 * 128])
        bgc = AL("bgc", [H, 1])
        rbrow = AL("rbrow", [1, NEXP])
        wgab = AL("wgab", [128, 8 * H], BF16)
        wrtf = AL("wrtf", [128, 8 * NEXP])
        ring = [AL(f"ring{i}", [128, 2048], BF16) for i in range(NRING)]

        PS8 = nc.alloc_psum_tensor("PS8", [128, 4096], F32)
        PSA = PS8[:, 0:2048]
        PSB = PS8[:, 2048:4096]

        def pbank(b, r0=0, r1=128, c0=0, c1=512):
            return PS8[r0:r1, b * 512 + c0:b * 512 + c1]

        XPQKY = ALS("XPQKY", [128, 16384], BF16)
        XPB = XPQKY
        QTW = lambda: XPQKY[:, 0:8 * W]
        KTW = lambda: XPQKY[:, 8 * W:16 * W]
        YTW = lambda: XPQKY[:, 16 * W:24 * W]
        KTOKW = ALS("KTOKW", [128, 4 * D], BF16)
        VW = ALS("VW", [128, 4 * D], BF16)
        gON = ALS("gON", [H, 1])
        gLOGG = ALS("gLOGG", [H, T])
        gCUM = ALS("gCUM", [H, T])
        SCR = ALS("SCR", [128, T])
        gLrel = SCR[0:H, 0:T]
        gAK = gLOGG
        gB = gCUM
        GSM = ALS("GSM", [H, 128])
        gLcs = GSM[0:H, 0:NCH]
        gAend = GSM[0:H, NCH:2 * NCH]
        gAQB = ALS("gAQB", [H, T], BF16)
        acols = ALS("acols", [128, 8 * NCH])
        Ssb = ALS("Ssb", [128, 8 * DKH])
        Sbb = ALS("Sbb", [128, 8 * DKH], BF16)
        trsb = ALS("trsb", [128, 32])
        vt_s = ALS("vt_s", [128, D], BF16)
        mka_s = SCR[0:128, 0:H * C]
        ach_s = ALS("ach_s", [128, H * C], BF16)
        aqbc2 = SCR[0:128, 1024:2048]
        prj_e = ALS("prj_e", [128, 512], BF16)
        prj_u = ALS("prj_u", [128, 512], BF16)

        ring_meta = {"n": 0, "free": []}

        def ring_load(src_ap):
            k = ring_meta["n"]
            ring_meta["n"] += 1
            slot = ring[k % NRING]
            hsem, hctr = ring_sems[k % NRING]
            nk = src_ap.shape[1]
            if k >= NRING:
                gp.wait_ge(_sT, ring_meta["free"][k - NRING])
            gp.dma_start(out=slot[:].rearrange("p (k f) -> p k f", k=nk),
                         in_=src_ap).then_inc(hsem, 16)
            return slot, (hsem, hctr.inc(16))

        def ring_done():
            ring_meta["free"].append(sT.v)

        for dst, src in ((sel, SEL), (mask, MASK), (idf, IDF), (one1f, ONE1F),
                         (selpf, SELPF), (bgc, BG), (rbrow, RB), (onesb, ONESB),
                         (one1b, ONE1B), (selpb, SELPB)):
            sy.dma_start(out=dst[:], in_=src[:]).then_inc(_sX, 16)
            sX.inc(16)
        sy.dma_start(out=wrtf[:].rearrange("p (k e) -> p k e", k=8),
                     in_=WRT[:].rearrange("(k p) e -> p k e", p=128)).then_inc(_sX, 16)
        sy.dma_start(out=R[:].rearrange("p (k t) -> p k t", k=8),
                     in_=XT[:].rearrange("(k p) t -> p k t", p=128)).then_inc(_sX, 16)
        sX.inc(32)
        X_CONST = sX.v

        gp.dma_start(out=XB[:].rearrange("p (k t) -> p k t", k=8),
                     in_=XT[:].rearrange("(k p) t -> p k t", p=128)).then_inc(_sW, 16)
        gp.dma_start(out=wgab[:].rearrange("p (k h) -> p k h", k=8),
                     in_=WGAT[:].rearrange("(k p) h -> p k h", p=128)).then_inc(_sW, 16)
        gp.dma_start(out=XPB[:].rearrange("p (k t) -> p k t", k=8),
                     in_=XPT[:].rearrange("(k p) t -> p k t", p=128)).then_inc(_sW, 16)
        sW.inc(48)
        W_INIT = sW.v

        def gates(xb, self_extras, first):
            te.wait_ge(_sW, W_INIT)
            te.wait_ge(_sX, X_CONST)
            for tc in range(4):
                for k in range(8):
                    te.matmul(PSA[0:H, tc * 512:(tc + 1) * 512],
                              wgab[:, k * H:(k + 1) * H],
                              xb[:, k * T + tc * 512:k * T + (tc + 1) * 512],
                              start=(k == 0), stop=(k == 7))
            te.inc_last(_sT, 1)
            tz = sT.inc()

            sc.wait_ge(_sT, tz)
            sc.activation(gLOGG[:], PSA[0:H, :], AF.Sigmoid, bias=bgc[:])
            sc.activation(gLOGG[:], gLOGG[:], AF.Ln)
            sg = sS.inc()

            ve.wait_ge(_sS, sg)
            if first:
                ve.memset(gON[:], 1.0)
            ve.tensor_tensor_scan(gCUM[:], gON[:].broadcast_to([H, T]), gLOGG[:],
                                  0.0, ALU.mult, ALU.add)
            ve.memset(gLcs[:, 0:1], 0.0)
            ve.tensor_copy(gLcs[:, 1:NCH],
                           gCUM[:].rearrange("h (c t) -> h c t", c=NCH)[:, 0:NCH - 1, C - 1])
            ve.tensor_tensor(gLrel.rearrange("h (c t) -> h c t", c=NCH),
                             gCUM[:].rearrange("h (c t) -> h c t", c=NCH),
                             gLcs.rearrange("h (c one) -> h c one", one=1)
                             .broadcast_to([H, NCH, C]),
                             ALU.subtract)
            lrel3 = gLrel.rearrange("h (c t) -> h c t", c=NCH)
            ve.tensor_tensor(gB[:].rearrange("h (c t) -> h c t", c=NCH), lrel3,
                             lrel3[:, :, C - 1:C].broadcast_to([H, NCH, C]),
                             ALU.subtract)
            vg = sV.inc()

            sc.wait_ge(_sV, vg)
            sc.activation(gAend.rearrange("h (c one) -> h c one", one=1),
                          lrel3[:, :, C - 1:C], AF.Exp)
            sc.activation(gB[:], gB[:], AF.Exp, scale=-1.0)
            if self_extras:
                sc.activation(gAK[:], gLrel, AF.Exp, scale=-1.0)
                sc.activation(gAQB[:], gLrel, AF.Exp)
            se = sS.inc()

            te.wait_ge(_sS, se)
            for p in range(8):
                te.matmul(pbank(p, c1=NCH),
                          selpf[:, p * 128:(p + 1) * 128], gAend,
                          start=True, stop=True)
            te.inc_last(_sT, 1)
            ta = sT.inc()

            ve.wait_ge(_sT, ta)
            ve.tensor_copy(acols[:].rearrange("p (b f) -> p b f", b=8),
                           PS8[:].rearrange("p (b f) -> p b f", b=8)[:, :, 0:NCH])
            va = sV.inc()
            te.wait_ge(_sS, se)
            te.wait_ge(_sV, va)
            return se, va

        bank_busy = {}

        def bank_wait_t(bk):
            if bk in bank_busy:
                te.wait_ge(_sV, bank_busy[bk])

        phi_prev = {"vm": None}

        def phi_evac(psv, dslice, pp, phi_scale, tmm):
            sc.wait_ge(_sT, tmm)
            if phi_prev["vm"] is not None:
                sc.wait_ge(_sV, phi_prev["vm"])
            sc.activation(prj_e[:], psv, AF.Exp, scale=phi_scale)
            sc.activation(prj_u[:], psv, AF.Relu, scale=phi_scale)
            smm = sS.inc()
            ve.wait_ge(_sS, smm)
            ve.scalar_tensor_tensor(dslice, prj_e[:], 1.0, prj_u[:],
                                    ALU.min, ALU.add)
            phi_prev["vm"] = sV.inc()
            return phi_prev["vm"]

        def copy_evac(psv, dslice, tmm):
            ve.wait_ge(_sT, tmm)
            ve.tensor_copy(dslice, psv)
            return sV.inc()

        def window_proj_feat(win, xb, whbm, dst, phi_scale):
            slabs = [ring_load(whbm[:, q * 256:(q + 1) * 256]
                               .rearrange("(k p) f -> p k f", p=128)) for q in range(4)]
            for grp, ft in enumerate(range(8)):
                slot, wld = slabs[ft // 2]
                bk = grp % 4
                te.wait_ge(wld[0], wld[1])
                bank_wait_t(bk)
                psv = PSA[:, bk * 512:(bk + 1) * 512]
                for k in range(8):
                    te.matmul(psv,
                              slot[:, k * 256 + (ft % 2) * 128:k * 256 + (ft % 2) * 128 + 128],
                              xb[:, k * T + win * W:k * T + (win + 1) * W],
                              start=(k == 0), stop=(k == 7))
                te.inc_last(_sT, 1)
                tmm = sT.inc()
                dslice = dst[:, ft * W:(ft + 1) * W]
                vm = (phi_evac(psv, dslice, 0, phi_scale, tmm) if phi_scale is not None
                      else copy_evac(psv, dslice, tmm))
                bank_busy[bk] = vm
            for _ in range(4):
                ring_done()

        def window_proj_tok(win, xb, whbm, dst, phi_scale):
            slabs = [ring_load(whbm[q * 256:(q + 1) * 256, :]
                               .rearrange("(k p) f -> p k f", p=128)) for q in range(4)]
            grp = 0
            for tt in range(4):
                t_abs = win * 4 + tt
                for half in range(2):
                    bk = grp % 4
                    for q in range(4):
                        te.wait_ge(slabs[q][1][0], slabs[q][1][1])
                    bank_wait_t(bk)
                    psv = PSA[:, bk * 512:(bk + 1) * 512]
                    for k in range(8):
                        slot = slabs[k // 2][0]
                        kin = k % 2
                        te.matmul(psv,
                                  xb[:, k * T + t_abs * 128:k * T + (t_abs + 1) * 128],
                                  slot[:, kin * 1024 + half * 512:kin * 1024 + (half + 1) * 512],
                                  start=(k == 0), stop=(k == 7))
                    te.inc_last(_sT, 1)
                    tmm = sT.inc()
                    dslice = dst[:, tt * D + half * 512:tt * D + (half + 1) * 512]
                    vm = (phi_evac(psv, dslice, 0, phi_scale, tmm) if phi_scale is not None
                          else copy_evac(psv, dslice, tmm))
                    bank_busy[bk] = vm
                    grp += 1
            for _ in range(4):
                ring_done()

        cb = {}
        PS3 = PS8[:].rearrange("p (b f) -> p b f", b=8)

        def chunk_P(c, cc, ktokw, vw, do_y, qtw=None, ktw=None, ytw=None):
            beta = (c % 2) * 64
            tt = cc // 2
            if c % 2 == 0:
                if "banks" in cb:
                    te.wait_ge(_sV, cb["banks"])
                if do_y:
                    te.transpose(pbank(6, c1=16), gAK[:, c * C:c * C + 128],
                                 idf[0:H, 0:H])
                te.transpose(pbank(7, c1=16), gB[:, c * C:c * C + 128],
                             idf[0:H, 0:H])
                te.inc_last(_sT, 1)
                ttr = sT.inc()
                ve.wait_ge(_sT, ttr)
                if do_y:
                    ve.tensor_copy(trsb[:, 0:16], pbank(6, c1=16))
                ve.tensor_copy(trsb[:, 16:32], pbank(7, c1=16))
                cb["banks"] = sV.inc()

            if do_y and c % 2 == 0:
                te.wait_ge(_sV, cb["banks"])
                for p in range(8):
                    te.matmul(pbank(p, c1=2 * C), selpb[:, p * 128:(p + 1) * 128],
                              gAQB[:, c * C:c * C + 2 * C], start=True, stop=True)
                te.inc_last(_sT, 1)
                taq = sT.inc()
                ve.wait_ge(_sT, taq)
                ve.tensor_copy(aqbc2.rearrange("p (q t) -> p q t", q=8),
                               PS3[:, :, 0:2 * C])
                cb["banks"] = sV.inc()
            if do_y:
                for j in range(2):
                    te.wait_ge(_sV, cb["banks"])
                    for p in range(8):
                        h = 8 * j + p
                        hb = 64 * (h % 2)
                        co = (h // 2) * W + cc * C
                        te.matmul(pbank(p, beta, beta + 64, 0, C),
                                  ktw[hb:hb + 64, co:co + C],
                                  qtw[hb:hb + 64, co:co + C],
                                  start=True, stop=True)
                    te.inc_last(_sT, 1)
                    tar = sT.inc()
                    ve.wait_ge(_sT, tar)
                    ve.tensor_tensor(
                        mka_s[beta:beta + 64, 0:512].rearrange("p (b t) -> p b t", b=8),
                        PS3[beta:beta + 64, :, 0:C],
                        trsb[beta:beta + 64, 8 * j:8 * j + 8]
                        .rearrange("p (b one) -> p b one", one=1)
                        .broadcast_to([64, 8, C]),
                        ALU.mult)
                    ve.tensor_tensor(
                        ach_s[beta:beta + 64, j * 512:(j + 1) * 512]
                        .rearrange("p (b t) -> p b t", b=8),
                        mka_s[beta:beta + 64, 0:512].rearrange("p (b t) -> p b t", b=8),
                        mask[beta:beta + 64, :]
                        .rearrange("p (one t) -> p one t", one=1)
                        .broadcast_to([64, 8, C]),
                        ALU.mult)
                    cb["banks"] = sV.inc()
                if "sbb_rd" in cb:
                    ve.wait_ge(_sT, cb["sbb_rd"])
                ve.tensor_copy(Sbb[:], Ssb[:])
                vsb = sV.inc()

            ve.tensor_tensor(vt_s[beta:beta + 64, :].rearrange("p (h j) -> p h j", h=H),
                             vw[beta:beta + 64, tt * D:(tt + 1) * D]
                             .rearrange("p (h j) -> p h j", h=H),
                             trsb[beta:beta + 64, 16:32]
                             .rearrange("p (h one) -> p h one", one=1)
                             .broadcast_to([64, H, DKH]),
                             ALU.mult)
            vvt = sV.inc()

            if do_y:
                te.wait_ge(_sV, cb["banks"])
                for h in range(H):
                    hb = 64 * (h % 2)
                    te.matmul(pbank(h // 2, hb, hb + 64, 0, C),
                              vw[beta:beta + 64, tt * D + h * DKH:tt * D + (h + 1) * DKH],
                              ach_s[beta:beta + 64, h * C:(h + 1) * C],
                              start=True, stop=True)
                te.inc_last(_sT, 1)
                tyi = sT.inc()
                ve.wait_ge(_sT, tyi)
                ve.tensor_tensor(
                    ytw[:].rearrange("p (q t) -> p q t", q=8)[:, :, cc * C:(cc + 1) * C],
                    PS3[:, :, 0:C],
                    aqbc2.rearrange("p (q t) -> p q t", q=8)[:, :, (c % 2) * C:(c % 2) * C + C],
                    ALU.mult)
                cb["banks"] = sV.inc()
                te.wait_ge(_sV, cb["banks"])
                te.wait_ge(_sV, vsb)
                for h in range(H):
                    hb = 64 * (h % 2)
                    co = (h // 2) * W + cc * C
                    te.matmul(pbank(h // 2, hb, hb + 64, 0, C),
                              Sbb[hb:hb + 64, (h // 2) * DKH:(h // 2 + 1) * DKH],
                              qtw[hb:hb + 64, co:co + C],
                              start=True, stop=True)
                te.inc_last(_sT, 1)
                tyc = sT.inc()
                cb["sbb_rd"] = tyc
                ve.wait_ge(_sT, tyc)
                ve.tensor_tensor(mka_s[:, 0:512].rearrange("p (q t) -> p q t", q=8),
                                 PS3[:, :, 0:C],
                                 aqbc2.rearrange("p (q t) -> p q t", q=8)[:, :, (c % 2) * C:(c % 2) * C + C],
                                 ALU.mult)
                ve.tensor_tensor(
                    ytw[:].rearrange("p (q t) -> p q t", q=8)[:, :, cc * C:(cc + 1) * C],
                    ytw[:].rearrange("p (q t) -> p q t", q=8)[:, :, cc * C:(cc + 1) * C],
                    mka_s[:, 0:512].rearrange("p (q t) -> p q t", q=8),
                    ALU.add)
                cb["banks"] = sV.inc()

            ve.tensor_tensor(Ssb[:].rearrange("p (q j) -> p q j", q=8),
                             Ssb[:].rearrange("p (q j) -> p q j", q=8),
                             acols[:].rearrange("p (q c) -> p q c", q=8)[:, :, c:c + 1]
                             .broadcast_to([128, 8, DKH]),
                             ALU.mult)
            vsm = sV.inc()
            te.wait_ge(_sV, cb["banks"])
            te.wait_ge(_sV, vvt)
            for h in range(H):
                hb = 64 * (h % 2)
                te.matmul(pbank(h // 2, hb, hb + 64, 0, DKH),
                          ktokw[beta:beta + 64, tt * D + h * DKH:tt * D + (h + 1) * DKH],
                          vt_s[beta:beta + 64, h * DKH:(h + 1) * DKH],
                          start=True, stop=True)
            te.inc_last(_sT, 1)
            tp = sT.inc()
            ve.wait_ge(_sT, tp)
            ve.wait_ge(_sV, vsm)
            ve.tensor_tensor(Ssb[:].rearrange("p (q j) -> p q j", q=8),
                             Ssb[:].rearrange("p (q j) -> p q j", q=8),
                             PS3[:, :, 0:DKH], ALU.add)
            cb["banks"] = sV.inc()

        gates(XPB, self_extras=False, first=True)
        ve.memset(Ssb[:], 0.0)
        sV.inc()

        for win in range(NWIN):
            for bk in range(4):
                bank_busy[bk] = max(bank_busy.get(bk, 0), cb.get("banks", 0))
            window_proj_tok(win, XPB, WKT, KTOKW, phi_scale=1.0)
            window_proj_tok(win, XPB, WVT, VW, phi_scale=None)
            if bank_busy:
                cb["banks"] = max(cb.get("banks", 0), max(bank_busy.values()))
            for cc in range(CPW):
                chunk_P(win * CPW + cc, cc, KTOKW, VW, do_y=False)

        ve.wait_ge(_sX, X_CONST)
        ve.tensor_scalar(Ssb[:], Ssb[:], sel[:], None, ALU.mult)
        v_sin = sV.inc()

        te.wait_ge(_sV, v_sin)
        gates(XB, self_extras=True, first=False)
        cb.clear()
        bank_busy.clear()

        qtw, ktw, ytw = QTW(), KTW(), YTW()
        for win in range(NWIN):
            for bk in range(4):
                bank_busy[bk] = max(bank_busy.get(bk, 0), cb.get("banks", 0))
            window_proj_feat(win, XB, WQT, qtw, phi_scale=SCALE)
            window_proj_feat(win, XB, WKT, ktw, phi_scale=1.0)
            window_proj_tok(win, XB, WKT, KTOKW, phi_scale=1.0)
            window_proj_tok(win, XB, WVT, VW, phi_scale=None)
            if bank_busy:
                cb["banks"] = max(cb.get("banks", 0), max(bank_busy.values()))
            for cc in range(CPW):
                chunk_P(win * CPW + cc, cc, KTOKW, VW, do_y=True,
                        qtw=qtw, ktw=ktw, ytw=ytw)
            slabs = [ring_load(WOT[:, q * 256:(q + 1) * 256]
                               .rearrange("(k p) f -> p k f", p=128)) for q in range(4)]
            te.wait_ge(_sV, cb.get("banks", 0))
            for grp, o in enumerate(range(8)):
                slot, wld = slabs[o // 2]
                bk = grp % 4
                te.wait_ge(wld[0], wld[1])
                bank_wait_t(bk)
                psv = PSA[:, bk * 512:(bk + 1) * 512]
                for k in range(8):
                    te.matmul(psv,
                              slot[:, k * 256 + (o % 2) * 128:k * 256 + (o % 2) * 128 + 128],
                              ytw[:, k * W:(k + 1) * W],
                              start=(k == 0), stop=(k == 7))
                te.inc_last(_sT, 1)
                tmm = sT.inc()
                ve.wait_ge(_sT, tmm)
                rsl = R[:, o * T + win * W:o * T + (win + 1) * W]
                ve.tensor_tensor(rsl, rsl, psv, ALU.add)
                bank_busy[bk] = sV.inc()
            for _ in range(4):
                ring_done()

        stack.close()
        HT = AL("HT", [128, 16 * T], BF16)
        sg_s = AL("sg_s", [128, T], BF16)
        cwbc = AL("cwbc", [128, T], BF16)
        epsc = AL("epsc", [128, 1])
        m_sb = AL("m_sb", [128, 512])
        v1_sb = AL("v1_sb", [128, 512])
        rstd_sb = AL("rstd_sb", [128, 512])
        rq_s = AL("rq_s", [128, 512], BF16)
        prb_s = AL("prb_s", [128, NEXP])
        top_s = AL("top_s", [128, 2 * NEXP])
        m12_s = AL("m12_s", [128, 2])
        cwTb = AL("cwTb", [NEXP, T], BF16)
        cwrow = AL("cwrow", [1, T], BF16)

        def layernorm(cast_out):
            ve.wait_ge(_sT, sT.v)
            ve.memset(epsc[:], EPS)
            for k in range(8):
                ve.tensor_copy(XB[:, k * T:(k + 1) * T], R[:, k * T:(k + 1) * T])
            vcast = sV.inc()
            prev_apply_v = None
            for tc in range(4):
                s1 = PSA[:, 0:512]
                s2 = PSA[:, 512:1024]
                sq_marks = []
                for k in range(8):
                    sqv = rq_s[:]
                    sc.wait_ge(_sV, vcast)
                    if sq_marks:
                        sc.wait_ge(_sT, sq_marks[-1])
                    if tc > 0 and k == 0:
                        sc.wait_ge(_sV, prev_apply_v)
                    sc.activation(sqv, XB[:, k * T + tc * 512:k * T + (tc + 1) * 512],
                                  AF.Square)
                    ssq = sS.inc()
                    te.wait_ge(_sV, vcast)
                    te.wait_ge(_sS, ssq)
                    if tc > 0 and k == 0:
                        te.wait_ge(_sV, prev_apply_v)
                    te.matmul(s1, onesb[:], XB[:, k * T + tc * 512:k * T + (tc + 1) * 512],
                              start=(k == 0), stop=(k == 7))
                    te.matmul(s2, onesb[:], sqv, start=(k == 0), stop=(k == 7))
                    te.inc_last(_sT, 1)
                    sq_marks.append(sT.inc())
                ve.wait_ge(_sT, sq_marks[-1])
                ve.tensor_scalar(m_sb[:], s1, 1.0 / D, None, ALU.mult)
                ve.tensor_tensor(v1_sb[:], m_sb[:], m_sb[:], ALU.mult)
                ve.scalar_tensor_tensor(v1_sb[:], s2, 1.0 / D, v1_sb[:],
                                        ALU.mult, ALU.subtract)
                vvar = sV.inc()
                sc.wait_ge(_sV, vvar)
                sc.activation(v1_sb[:], v1_sb[:], AF.Sqrt, bias=epsc[:])
                ssm = sS.inc()
                ve.wait_ge(_sS, ssm)
                ve.reciprocal(rstd_sb[:], v1_sb[:])
                for k in range(8):
                    rsl = R[:, k * T + tc * 512:k * T + (tc + 1) * 512]
                    ve.tensor_tensor(rsl, rsl, m_sb[:], ALU.subtract)
                    ve.tensor_tensor(rsl, rsl, rstd_sb[:], ALU.mult)
                prev_apply_v = sV.inc()
            if cast_out:
                ve.wait_ge(_sV, prev_apply_v)
                for k in range(8):
                    ve.tensor_copy(XB[:, k * T:(k + 1) * T], R[:, k * T:(k + 1) * T])
                return sV.inc()
            return prev_apply_v

        v_x1b = layernorm(cast_out=True)

        tr_prev = {}
        for tt in range(16):
            pp = tt % 2
            psl = pbank(4 + 2 * pp, c1=NEXP)
            pst = pbank(5 + 2 * pp, c1=128)
            te.wait_ge(_sV, v_x1b)
            if pp in tr_prev:
                te.wait_ge(_sV, tr_prev[pp])
            for k in range(8):
                te.matmul(psl, R[:, k * T + tt * 128:k * T + (tt + 1) * 128],
                          wrtf[:, k * NEXP:(k + 1) * NEXP],
                          start=(k == 0), stop=False)
            te.matmul(psl, one1f[:], rbrow[:], start=False, stop=True)
            te.inc_last(_sT, 1)
            tl = sT.inc()
            sc.wait_ge(_sT, tl)
            if tr_prev:
                sc.wait_ge(_sV, max(tr_prev.values()))
            sc.activation(prb_s[:], psl, AF.Sigmoid)
            sprb = sS.inc()
            ve.wait_ge(_sS, sprb)
            ve.reduce_max(m12_s[:, 0:1], prb_s[:], axis=AXX)
            ve.tensor_scalar(top_s[:, 0:NEXP], prb_s[:], m12_s[:, 0:1], None, ALU.is_equal)
            ve.scalar_tensor_tensor(top_s[:, NEXP:2 * NEXP], top_s[:, 0:NEXP], -1e30,
                                    prb_s[:], ALU.mult, ALU.add)
            ve.reduce_max(m12_s[:, 1:2], top_s[:, NEXP:2 * NEXP], axis=AXX)
            ve.tensor_scalar(top_s[:, 0:NEXP], prb_s[:], m12_s[:, 1:2], None, ALU.is_ge)
            ve.tensor_tensor(prb_s[:], prb_s[:], top_s[:, 0:NEXP], ALU.mult)
            vcw = sV.inc()
            te.wait_ge(_sV, vcw)
            te.transpose(pst[0:NEXP, 0:128], prb_s[:], idf[:]).then_inc(_sT, 1)
            tct = sT.inc()
            ve.wait_ge(_sT, tct)
            ve.tensor_copy(cwTb[:, tt * 128:(tt + 1) * 128], pst[0:NEXP, 0:128])
            tr_prev[pp] = sV.inc()
        v_cwt = max(tr_prev.values())

        fb = {}

        def expert(e):
            wg_hbm = WGS if e == -1 else WGE[e]
            wu_hbm = WUS if e == -1 else WUE[e]
            wd_hbm = WDS if e == -1 else WDE[e]
            prod_marks = []
            for q in range(8):
                gsl = ring_load(wg_hbm[:, q * 256:(q + 1) * 256]
                                .rearrange("(k p) i -> p k i", p=128))
                usl = ring_load(wu_hbm[:, q * 256:(q + 1) * 256]
                                .rearrange("(k p) i -> p k i", p=128))
                for ii in range(2):
                    i_t = 2 * q + ii
                    te.wait_ge(gsl[1][0], gsl[1][1])
                    te.wait_ge(_sV, v_cwt)
                    if "d_bank" in fb:
                        te.wait_ge(_sV, max(fb["d_bank"].values()))
                        fb.pop("d_bank")
                    tg = {}
                    for tc in range(4):
                        if "g_bank" in fb:
                            te.wait_ge(_sS, fb["g_bank"][tc])
                        for k in range(8):
                            te.matmul(PSA[:, tc * 512:(tc + 1) * 512],
                                      gsl[0][:, k * 256 + ii * 128:k * 256 + ii * 128 + 128],
                                      XB[:, k * T + tc * 512:k * T + (tc + 1) * 512],
                                      start=(k == 0), stop=(k == 7))
                        te.inc_last(_sT, 1)
                        tg[tc] = sT.inc()
                    te.wait_ge(usl[1][0], usl[1][1])
                    tu = {}
                    for tc in range(4):
                        if "u_bank" in fb:
                            te.wait_ge(_sV, fb["u_bank"][tc])
                        for k in range(8):
                            te.matmul(PSB[:, tc * 512:(tc + 1) * 512],
                                      usl[0][:, k * 256 + ii * 128:k * 256 + ii * 128 + 128],
                                      XB[:, k * T + tc * 512:k * T + (tc + 1) * 512],
                                      start=(k == 0), stop=(k == 7))
                        te.inc_last(_sT, 1)
                        tu[tc] = sT.inc()
                    gb = {}
                    for tc in range(4):
                        sc.wait_ge(_sT, tg[tc])
                        if "sg_scr" in fb:
                            sc.wait_ge(_sV, fb["sg_scr"][tc])
                        sc.activation(sg_s[:, tc * 512:(tc + 1) * 512],
                                      PSA[:, tc * 512:(tc + 1) * 512], AF.Silu)
                        gb[tc] = sS.inc()
                    fb["g_bank"] = gb
                    ub = {}
                    for tc in range(4):
                        ve.wait_ge(_sT, tu[tc])
                        ve.wait_ge(_sS, gb[tc])
                        if "ht_rd" in fb:
                            ve.wait_ge(_sT, fb["ht_rd"])
                        if e >= 0:
                            ve.wait_ge(_sV, fb["cwbc_ok"])
                        ve.tensor_tensor(HT[:, i_t * T + tc * 512:i_t * T + (tc + 1) * 512],
                                         sg_s[:, tc * 512:(tc + 1) * 512],
                                         PSB[:, tc * 512:(tc + 1) * 512], ALU.mult)
                        if e >= 0:
                            ve.tensor_tensor(HT[:, i_t * T + tc * 512:i_t * T + (tc + 1) * 512],
                                             HT[:, i_t * T + tc * 512:i_t * T + (tc + 1) * 512],
                                             cwbc[:, tc * 512:(tc + 1) * 512], ALU.mult)
                        ub[tc] = sV.inc()
                    fb["u_bank"] = ub
                    fb["sg_scr"] = ub
                    prod_marks.append(ub[3])
                ring_done()
                ring_done()
            h_ready_v = prod_marks[-1]
            last_td = None
            for q in range(8):
                dsl = ring_load(wd_hbm[:, q * 128:(q + 1) * 128]
                                .rearrange("(k p) o -> p k o", p=128))
                for oo in range(1):
                    o_t = q
                    te.wait_ge(dsl[1][0], dsl[1][1])
                    te.wait_ge(_sV, h_ready_v)
                    if "g_bank" in fb:
                        te.wait_ge(_sS, max(fb["g_bank"].values()))
                        fb.pop("g_bank")
                    td = {}
                    for tc in range(4):
                        if "d_bank" in fb:
                            te.wait_ge(_sV, fb["d_bank"][tc])
                        for ki in range(16):
                            te.matmul(PSA[:, tc * 512:(tc + 1) * 512],
                                      dsl[0][:, ki * 128:ki * 128 + 128],
                                      HT[:, ki * T + tc * 512:ki * T + (tc + 1) * 512],
                                      start=(ki == 0), stop=(ki == 15))
                        te.inc_last(_sT, 1)
                        td[tc] = sT.inc()
                    last_td = td[3]
                    db = {}
                    for tc in range(4):
                        ve.wait_ge(_sT, td[tc])
                        rsl = R[:, o_t * T + tc * 512:o_t * T + (tc + 1) * 512]
                        ve.tensor_tensor(rsl, rsl, PSA[:, tc * 512:(tc + 1) * 512], ALU.add)
                        db[tc] = sV.inc()
                    fb["d_bank"] = db
                ring_done()
            fb["ht_rd"] = last_td
            ne = e + 1
            if ne <= NEXP - 1:
                gp.wait_ge(_sV, v_cwt)
                gp.wait_ge(_sW, sW.v)
                if "cwrow_rd" in fb:
                    gp.wait_ge(_sT, fb["cwrow_rd"])
                gp.dma_start(out=cwrow[:], in_=cwTb[ne:ne + 1, :]).then_inc(_sW, 16)
                wrow = sW.inc(16)
                te.wait_ge(_sW, wrow)
                if "u_bank" in fb:
                    te.wait_ge(_sV, max(fb["u_bank"].values()))
                for tc in range(4):
                    te.matmul(PSB[:, tc * 512:(tc + 1) * 512], one1b[:],
                              cwrow[0:1, tc * 512:(tc + 1) * 512],
                              start=True, stop=True)
                te.inc_last(_sT, 1)
                tcb = sT.inc()
                fb["cwrow_rd"] = tcb
                ve.wait_ge(_sT, tcb)
                for tc in range(4):
                    ve.tensor_copy(cwbc[:, tc * 512:(tc + 1) * 512],
                                   PSB[:, tc * 512:(tc + 1) * 512])
                m = sV.inc()
                fb["u_bank"] = {tc: m for tc in range(4)}
                fb["cwbc_ok"] = m

        for e in range(-1, NEXP):
            expert(e)

        layernorm(cast_out=False)
        v_done = sV.v

        sy.wait_ge(_sV, v_done)
        sy.dma_start(out=OUT[:].rearrange("(k p) t -> p k t", p=128),
                     in_=R[:].rearrange("p (k t) -> p k t", k=8)).then_inc(_sO, 16)
        sO.inc(16)
        sy.wait_ge(_sO, sO.v)

        @block.sync
        def _(eng):
            sy.replay(eng)

        @block.gpsimd
        def _(eng):
            gp.replay(eng)

        @block.tensor
        def _(eng):
            te.replay(eng)

        @block.vector
        def _(eng):
            ve.replay(eng)

        @block.scalar
        def _(eng):
            sc.replay(eng)

    return nc


_NC_CACHE = {}


def _get_nc():
    if "nc" not in _NC_CACHE:
        _NC_CACHE["nc"] = build_program()
    return _NC_CACHE["nc"]


def _consts():
    maskq = np.triu(np.ones((C, C), np.float32))
    MASKc = np.ascontiguousarray(np.concatenate([maskq, maskq], axis=0))
    IDFc = np.eye(128, dtype=np.float32)
    ONESBc = np.ones((128, 128), dtype=np.float32).astype(BFNP)
    ONE1Bc = np.ones((1, 128), dtype=np.float32).astype(BFNP)
    ONE1Fc = np.ones((1, 128), dtype=np.float32)
    selp = np.zeros((H, 8 * 128), np.float32)
    for p in range(8):
        selp[2 * p, p * 128:p * 128 + 64] = 1.0
        selp[2 * p + 1, p * 128 + 64:(p + 1) * 128] = 1.0
    return MASKc, IDFc, ONESBc, ONE1Bc, ONE1Fc, selp.astype(BFNP), selp


def kernel(L0_attn_x, L0_attn_Wq, L0_attn_Wk, L0_attn_Wv, L0_attn_Wo,
           L0_attn_Wg, L0_attn_bg, L0_ffn_Wg_s, L0_ffn_Wu_s, L0_ffn_Wd_s,
           L0_ffn_Wg, L0_ffn_Wu, L0_ffn_Wd, L0_ffn_Wr, L0_ffn_rb):
    x = np.ascontiguousarray(np.asarray(L0_attn_x, np.float32)[0])

    def tp(w):
        return np.ascontiguousarray(np.asarray(w, np.float32).T)

    MASKc, IDFc, ONESBc, ONE1Bc, ONE1Fc, SELPBc, SELPFc = _consts()
    common = dict(
        WQT=tp(L0_attn_Wq), WKT=tp(L0_attn_Wk), WVT=tp(L0_attn_Wv),
        WOT=tp(L0_attn_Wo), WGAT=tp(L0_attn_Wg),
        BG=np.ascontiguousarray(np.asarray(L0_attn_bg, np.float32)[:, None]),
        WRT=tp(L0_ffn_Wr),
        RB=np.ascontiguousarray(np.asarray(L0_ffn_rb, np.float32)[None, :]),
        WGS=tp(L0_ffn_Wg_s), WUS=tp(L0_ffn_Wu_s), WDS=tp(L0_ffn_Wd_s),
        WGE=np.ascontiguousarray(np.transpose(np.asarray(L0_ffn_Wg, np.float32), (0, 2, 1))),
        WUE=np.ascontiguousarray(np.transpose(np.asarray(L0_ffn_Wu, np.float32), (0, 2, 1))),
        WDE=np.ascontiguousarray(np.transpose(np.asarray(L0_ffn_Wd, np.float32), (0, 2, 1))),
        MASK=MASKc, IDF=IDFc, ONESB=ONESBc, ONE1B=ONE1Bc, ONE1F=ONE1Fc,
        SELPB=SELPBc, SELPF=SELPFc,
    )
    xts = [np.ascontiguousarray(x[j * T:(j + 1) * T].T) for j in range(8)]
    in_maps = []
    for j in range(8):
        m = dict(common)
        m["XT"] = xts[j]
        m["XPT"] = xts[j - 1] if j > 0 else xts[0]
        m["SEL"] = np.full((128, 1), 0.0 if j == 0 else 1.0, np.float32)
        in_maps.append(m)

    nc = _get_nc()
    res = run_bass_kernel_spmd(nc, in_maps, core_ids=list(range(8))).results
    out = np.concatenate([np.asarray(r["OUT"], np.float32).T for r in res], axis=0)
    return np.ascontiguousarray(out[None])


# version 10 (final)
# speedup vs baseline: 1.1846x; 1.1846x over previous
import contextlib
import numpy as np
import ml_dtypes
import concourse.bass as bass
import concourse.mybir as mybir
from concourse.bass_utils import run_bass_kernel_spmd

F32 = mybir.dt.float32
BF16 = mybir.dt.bfloat16
AF = mybir.ActivationFunctionType
ALU = mybir.AluOpType
AXX = mybir.AxisListType.X
BFNP = ml_dtypes.bfloat16

H = 16
DKH = 64
D = 1024
T = 2048
C = 64
NCH = T // C
W = 512
NWIN = T // W
CPW = W // C
IFF = 2048
NEXP = 8
SCALE = float(1.0 / np.sqrt(np.float32(DKH)))
EPS = 1e-5
NRING = 4


class Rec:

    class Inst:
        def __init__(self, rec):
            self.rec = rec

        def then_inc(self, sem, val=1):
            self.rec.ops.append(("__then_inc", (sem, val), {}))
            return self

    def __init__(self, name):
        self.__dict__["name"] = name
        self.__dict__["ops"] = []
        self.__dict__["fsem"] = None
        self.__dict__["fcount"] = 0

    def set_fence(self, sem):
        self.__dict__["fsem"] = sem

    @property
    def fc(self):
        return self.__dict__["fcount"]

    def inc_last(self, sem, val=1):
        self.__dict__["ops"].append(("__then_inc", (sem, val), {}))

    def __getattr__(self, meth):
        d = self.__dict__
        ops = d["ops"]

        def call(*a, **k):
            real = meth not in ("wait_ge", "nop")
            if d["fsem"] is not None and real:
                if d["fcount"] > 0:
                    ops.append(("wait_ge", (d["fsem"], d["fcount"]), {}))
                ops.append((meth, a, k))
                ops.append(("__then_inc", (d["fsem"], 1), {}))
                d["fcount"] += 1
            else:
                ops.append((meth, a, k))
            return Rec.Inst(self)

        return call

    def replay(self, eng):
        last = None
        for meth, a, k in self.__dict__["ops"]:
            if meth == "__then_inc":
                last.then_inc(*a)
            else:
                last = getattr(eng, meth)(*a, **k)


class Ctr:
    def __init__(self, handle):
        self.h = handle
        self.v = 0

    def inc(self, n=1):
        self.v += n
        return self.v


def build_program():
    nc = bass.Bass("TRN2", target_bir_lowering=False)

    def din(name, shape, dtype=F32):
        return nc.dram_tensor(name, shape, dtype, kind="ExternalInput")

    XT = din("XT", [D, T])
    XPT = din("XPT", [D, T])
    SEL = din("SEL", [128, 1])
    WQT = din("WQT", [D, D])
    WKT = din("WKT", [D, D])
    WVT = din("WVT", [D, D])
    WOT = din("WOT", [D, D])
    WGAT = din("WGAT", [D, H])
    BG = din("BG", [H, 1])
    WRT = din("WRT", [D, NEXP])
    RB = din("RB", [1, NEXP])
    WGS = din("WGS", [D, IFF])
    WUS = din("WUS", [D, IFF])
    WDS = din("WDS", [IFF, D])
    WGE = din("WGE", [NEXP, D, IFF])
    WUE = din("WUE", [NEXP, D, IFF])
    WDE = din("WDE", [NEXP, IFF, D])
    MASK = din("MASK", [128, C])
    IDF = din("IDF", [128, 128])
    ONESB = din("ONESB", [128, 128], BF16)
    ONE1B = din("ONE1B", [1, 128], BF16)
    ONE1F = din("ONE1F", [1, 128])
    SELPB = din("SELPB", [H, 8 * 128], BF16)
    SELPF = din("SELPF", [H, 8 * 128])
    OUT = nc.dram_tensor("OUT", [D, T], F32, kind="ExternalOutput")

    sy, gp, te, ve, sc = (Rec(n) for n in ("sync", "gpsimd", "tensor", "vector", "scalar"))

    stack = contextlib.ExitStack()
    with (
        nc.semaphore("sW") as _sW,
        nc.semaphore("sWA") as _sWA,
        nc.semaphore("sWB") as _sWB,
        nc.semaphore("sWC") as _sWC,
        nc.semaphore("sWD") as _sWD,
        nc.semaphore("sX") as _sX,
        nc.semaphore("sT") as _sT,
        nc.semaphore("sV") as _sV,
        nc.semaphore("sS") as _sS,
        nc.semaphore("sO") as _sO,
        nc.Block() as block,
    ):
        sW, sWA, sWB, sWC, sWD, sX, sT, sO = (
            Ctr(x) for x in (_sW, _sWA, _sWB, _sWC, _sWD, _sX, _sT, _sO))
        ring_sems = [(_sWA, sWA), (_sWB, sWB), (_sWC, sWC), (_sWD, sWD)]
        ve.set_fence(_sV)
        sc.set_fence(_sS)

        class FC:
            def __init__(self, rec):
                self.rec = rec

            def inc(self, n=1):
                return self.rec.fc

            @property
            def v(self):
                return self.rec.fc

        sV = FC(ve)
        sS = FC(sc)

        def AL(name, shape, dtype=F32):
            return nc.alloc_sbuf_tensor(name, shape, dtype)

        def ALS(name, shape, dtype=F32):
            return stack.enter_context(nc.sbuf_tensor(name, shape, dtype))

        R = AL("R", [128, 8 * T])
        XB = AL("XB", [128, 8 * T], BF16)
        sel = AL("sel", [128, 1])
        mask = AL("mask", [128, C])
        idf = AL("idf", [128, 128])
        onesb = AL("onesb", [128, 128], BF16)
        one1b = AL("one1b", [1, 128], BF16)
        one1f = AL("one1f", [1, 128])
        selpb = AL("selpb", [H, 8 * 128], BF16)
        selpf = AL("selpf", [H, 8 * 128])
        bgc = AL("bgc", [H, 1])
        rbrow = AL("rbrow", [1, NEXP])
        wgab = AL("wgab", [128, 8 * H], BF16)
        wrtf = AL("wrtf", [128, 8 * NEXP])
        ring = [AL(f"ring{i}", [128, 2048], BF16) for i in range(NRING)]

        PS8 = nc.alloc_psum_tensor("PS8", [128, 4096], F32)
        PSA = PS8[:, 0:2048]
        PSB = PS8[:, 2048:4096]

        def pbank(b, r0=0, r1=128, c0=0, c1=512):
            return PS8[r0:r1, b * 512 + c0:b * 512 + c1]

        XPQKY = ALS("XPQKY", [128, 16384], BF16)
        XPB = XPQKY
        QTW = lambda: XPQKY[:, 0:8 * W]
        KTW = lambda: XPQKY[:, 8 * W:16 * W]
        YTW = lambda: XPQKY[:, 16 * W:24 * W]
        KTOKW = ALS("KTOKW", [128, 4 * D], BF16)
        VW = ALS("VW", [128, 4 * D], BF16)
        gON = ALS("gON", [H, 1])
        gLOGG = ALS("gLOGG", [H, T])
        gCUM = ALS("gCUM", [H, T])
        SCR = ALS("SCR", [128, T])
        gLrel = SCR[0:H, 0:T]
        gAK = gLOGG
        gB = gCUM
        GSM = ALS("GSM", [H, 128])
        gLcs = GSM[0:H, 0:NCH]
        gAend = GSM[0:H, NCH:2 * NCH]
        gAQB = ALS("gAQB", [H, T], BF16)
        acols = ALS("acols", [128, 8 * NCH])
        Ssb = ALS("Ssb", [128, 8 * DKH])
        Sbb = ALS("Sbb", [128, 8 * DKH], BF16)
        trsb = ALS("trsb", [128, 32])
        vt_s = ALS("vt_s", [128, D], BF16)
        mka_s = SCR[0:128, 0:H * C]
        ach_s = ALS("ach_s", [128, H * C], BF16)
        aqbc2 = SCR[0:128, 1024:2048]
        prj_e = ALS("prj_e", [128, 512], BF16)
        prj_u = ALS("prj_u", [128, 512], BF16)

        ring_meta = {"n": 0, "free": []}

        def ring_load(src_ap):
            k = ring_meta["n"]
            ring_meta["n"] += 1
            slot = ring[k % NRING]
            hsem, hctr = ring_sems[k % NRING]
            nk = src_ap.shape[1]
            if k >= NRING:
                gp.wait_ge(_sT, ring_meta["free"][k - NRING])
            gp.dma_start(out=slot[:].rearrange("p (k f) -> p k f", k=nk),
                         in_=src_ap).then_inc(hsem, 16)
            return slot, (hsem, hctr.inc(16))

        def ring_done():
            ring_meta["free"].append(sT.v)

        for dst, src in ((sel, SEL), (mask, MASK), (idf, IDF), (one1f, ONE1F),
                         (selpf, SELPF), (bgc, BG), (rbrow, RB), (onesb, ONESB),
                         (one1b, ONE1B), (selpb, SELPB)):
            sy.dma_start(out=dst[:], in_=src[:]).then_inc(_sX, 16)
            sX.inc(16)
        sy.dma_start(out=wrtf[:].rearrange("p (k e) -> p k e", k=8),
                     in_=WRT[:].rearrange("(k p) e -> p k e", p=128)).then_inc(_sX, 16)
        sy.dma_start(out=R[:].rearrange("p (k t) -> p k t", k=8),
                     in_=XT[:].rearrange("(k p) t -> p k t", p=128)).then_inc(_sX, 16)
        sX.inc(32)
        X_CONST = sX.v

        gp.dma_start(out=XB[:].rearrange("p (k t) -> p k t", k=8),
                     in_=XT[:].rearrange("(k p) t -> p k t", p=128)).then_inc(_sW, 16)
        gp.dma_start(out=wgab[:].rearrange("p (k h) -> p k h", k=8),
                     in_=WGAT[:].rearrange("(k p) h -> p k h", p=128)).then_inc(_sW, 16)
        gp.dma_start(out=XPB[:].rearrange("p (k t) -> p k t", k=8),
                     in_=XPT[:].rearrange("(k p) t -> p k t", p=128)).then_inc(_sW, 16)
        sW.inc(48)
        W_INIT = sW.v

        def gates(xb, self_extras, first):
            te.wait_ge(_sW, W_INIT)
            te.wait_ge(_sX, X_CONST)
            for tc in range(4):
                for k in range(8):
                    te.matmul(PSA[0:H, tc * 512:(tc + 1) * 512],
                              wgab[:, k * H:(k + 1) * H],
                              xb[:, k * T + tc * 512:k * T + (tc + 1) * 512],
                              start=(k == 0), stop=(k == 7))
            te.inc_last(_sT, 1)
            tz = sT.inc()

            sc.wait_ge(_sT, tz)
            sc.activation(gLOGG[:], PSA[0:H, :], AF.Sigmoid, bias=bgc[:])
            sc.activation(gLOGG[:], gLOGG[:], AF.Ln)
            sg = sS.inc()

            ve.wait_ge(_sS, sg)
            if first:
                ve.memset(gON[:], 1.0)
            ve.tensor_tensor_scan(gCUM[:], gON[:].broadcast_to([H, T]), gLOGG[:],
                                  0.0, ALU.mult, ALU.add)
            ve.memset(gLcs[:, 0:1], 0.0)
            ve.tensor_copy(gLcs[:, 1:NCH],
                           gCUM[:].rearrange("h (c t) -> h c t", c=NCH)[:, 0:NCH - 1, C - 1])
            ve.tensor_tensor(gLrel.rearrange("h (c t) -> h c t", c=NCH),
                             gCUM[:].rearrange("h (c t) -> h c t", c=NCH),
                             gLcs.rearrange("h (c one) -> h c one", one=1)
                             .broadcast_to([H, NCH, C]),
                             ALU.subtract)
            lrel3 = gLrel.rearrange("h (c t) -> h c t", c=NCH)
            ve.tensor_tensor(gB[:].rearrange("h (c t) -> h c t", c=NCH), lrel3,
                             lrel3[:, :, C - 1:C].broadcast_to([H, NCH, C]),
                             ALU.subtract)
            vg = sV.inc()

            sc.wait_ge(_sV, vg)
            sc.activation(gAend.rearrange("h (c one) -> h c one", one=1),
                          lrel3[:, :, C - 1:C], AF.Exp)
            sc.activation(gB[:], gB[:], AF.Exp, scale=-1.0)
            if self_extras:
                sc.activation(gAK[:], gLrel, AF.Exp, scale=-1.0)
                sc.activation(gAQB[:], gLrel, AF.Exp)
            se = sS.inc()

            te.wait_ge(_sS, se)
            for p in range(8):
                te.matmul(pbank(p, c1=NCH),
                          selpf[:, p * 128:(p + 1) * 128], gAend,
                          start=True, stop=True)
            te.inc_last(_sT, 1)
            ta = sT.inc()

            ve.wait_ge(_sT, ta)
            ve.tensor_copy(acols[:].rearrange("p (b f) -> p b f", b=8),
                           PS8[:].rearrange("p (b f) -> p b f", b=8)[:, :, 0:NCH])
            va = sV.inc()
            te.wait_ge(_sS, se)
            te.wait_ge(_sV, va)
            return se, va

        bank_busy = {}

        def bank_wait_t(bk):
            if bk in bank_busy:
                te.wait_ge(_sV, bank_busy[bk])

        phi_prev = {"vm": None}

        def phi_evac(psv, dslice, pp, phi_scale, tmm):
            sc.wait_ge(_sT, tmm)
            if phi_prev["vm"] is not None:
                sc.wait_ge(_sV, phi_prev["vm"])
            sc.activation(prj_e[:], psv, AF.Exp, scale=phi_scale)
            sc.activation(prj_u[:], psv, AF.Relu, scale=phi_scale)
            smm = sS.inc()
            ve.wait_ge(_sS, smm)
            ve.scalar_tensor_tensor(dslice, prj_e[:], 1.0, prj_u[:],
                                    ALU.min, ALU.add)
            phi_prev["vm"] = sV.inc()
            return phi_prev["vm"]

        def copy_evac(psv, dslice, tmm):
            ve.wait_ge(_sT, tmm)
            ve.tensor_copy(dslice, psv)
            return sV.inc()

        def window_proj_feat(win, xb, whbm, dst, phi_scale):
            slabs = [ring_load(whbm[:, q * 256:(q + 1) * 256]
                               .rearrange("(k p) f -> p k f", p=128)) for q in range(4)]
            for grp, ft in enumerate(range(8)):
                slot, wld = slabs[ft // 2]
                bk = grp % 4
                te.wait_ge(wld[0], wld[1])
                bank_wait_t(bk)
                psv = PSA[:, bk * 512:(bk + 1) * 512]
                for k in range(8):
                    te.matmul(psv,
                              slot[:, k * 256 + (ft % 2) * 128:k * 256 + (ft % 2) * 128 + 128],
                              xb[:, k * T + win * W:k * T + (win + 1) * W],
                              start=(k == 0), stop=(k == 7))
                te.inc_last(_sT, 1)
                tmm = sT.inc()
                dslice = dst[:, ft * W:(ft + 1) * W]
                vm = (phi_evac(psv, dslice, 0, phi_scale, tmm) if phi_scale is not None
                      else copy_evac(psv, dslice, tmm))
                bank_busy[bk] = vm
            for _ in range(4):
                ring_done()

        def window_proj_tok(win, xb, whbm, dst, phi_scale):
            slabs = [ring_load(whbm[q * 256:(q + 1) * 256, :]
                               .rearrange("(k p) f -> p k f", p=128)) for q in range(4)]
            grp = 0
            for tt in range(4):
                t_abs = win * 4 + tt
                for half in range(2):
                    bk = grp % 4
                    for q in range(4):
                        te.wait_ge(slabs[q][1][0], slabs[q][1][1])
                    bank_wait_t(bk)
                    psv = PSA[:, bk * 512:(bk + 1) * 512]
                    for k in range(8):
                        slot = slabs[k // 2][0]
                        kin = k % 2
                        te.matmul(psv,
                                  xb[:, k * T + t_abs * 128:k * T + (t_abs + 1) * 128],
                                  slot[:, kin * 1024 + half * 512:kin * 1024 + (half + 1) * 512],
                                  start=(k == 0), stop=(k == 7))
                    te.inc_last(_sT, 1)
                    tmm = sT.inc()
                    dslice = dst[:, tt * D + half * 512:tt * D + (half + 1) * 512]
                    vm = (phi_evac(psv, dslice, 0, phi_scale, tmm) if phi_scale is not None
                          else copy_evac(psv, dslice, tmm))
                    bank_busy[bk] = vm
                    grp += 1
            for _ in range(4):
                ring_done()

        cb = {}
        PS3 = PS8[:].rearrange("p (b f) -> p b f", b=8)

        def chunk_P(c, cc, ktokw, vw, do_y, qtw=None, ktw=None, ytw=None):
            beta = (c % 2) * 64
            tt = cc // 2
            if c % 2 == 0:
                if "banks" in cb:
                    te.wait_ge(_sV, cb["banks"])
                if do_y:
                    te.transpose(pbank(6, c1=16), gAK[:, c * C:c * C + 128],
                                 idf[0:H, 0:H])
                te.transpose(pbank(7, c1=16), gB[:, c * C:c * C + 128],
                             idf[0:H, 0:H])
                te.inc_last(_sT, 1)
                ttr = sT.inc()
                ve.wait_ge(_sT, ttr)
                if do_y:
                    ve.tensor_copy(trsb[:, 0:16], pbank(6, c1=16))
                ve.tensor_copy(trsb[:, 16:32], pbank(7, c1=16))
                cb["banks"] = sV.inc()

            if do_y and c % 2 == 0:
                te.wait_ge(_sV, cb["banks"])
                for p in range(8):
                    te.matmul(pbank(p, c1=2 * C), selpb[:, p * 128:(p + 1) * 128],
                              gAQB[:, c * C:c * C + 2 * C], start=True, stop=True)
                te.inc_last(_sT, 1)
                taq = sT.inc()
                ve.wait_ge(_sT, taq)
                ve.tensor_copy(aqbc2.rearrange("p (q t) -> p q t", q=8),
                               PS3[:, :, 0:2 * C])
                cb["banks"] = sV.inc()
            if do_y and c % 2 == 0:
                for j in range(2):
                    te.wait_ge(_sV, cb["banks"])
                    for p in range(8):
                        h = 8 * j + p
                        hb = 64 * (h % 2)
                        for par in range(2):
                            co = (h // 2) * W + (cc + par) * C
                            te.matmul(pbank(p, 64 * par, 64 * par + 64, 0, C),
                                      ktw[hb:hb + 64, co:co + C],
                                      qtw[hb:hb + 64, co:co + C],
                                      start=True, stop=True)
                    te.inc_last(_sT, 1)
                    tar = sT.inc()
                    ve.wait_ge(_sT, tar)
                    ve.tensor_tensor(
                        mka_s[:, 0:512].rearrange("p (b t) -> p b t", b=8),
                        PS3[:, :, 0:C],
                        trsb[:, 8 * j:8 * j + 8]
                        .rearrange("p (b one) -> p b one", one=1)
                        .broadcast_to([128, 8, C]),
                        ALU.mult)
                    ve.tensor_tensor(
                        ach_s[:, j * 512:(j + 1) * 512]
                        .rearrange("p (b t) -> p b t", b=8),
                        mka_s[:, 0:512].rearrange("p (b t) -> p b t", b=8),
                        mask[:, :]
                        .rearrange("p (one t) -> p one t", one=1)
                        .broadcast_to([128, 8, C]),
                        ALU.mult)
                    cb["banks"] = sV.inc()
            if do_y:
                if "sbb_rd" in cb:
                    ve.wait_ge(_sT, cb["sbb_rd"])
                ve.tensor_copy(Sbb[:], Ssb[:])
                vsb = sV.inc()

            ve.tensor_tensor(vt_s[beta:beta + 64, :].rearrange("p (h j) -> p h j", h=H),
                             vw[beta:beta + 64, tt * D:(tt + 1) * D]
                             .rearrange("p (h j) -> p h j", h=H),
                             trsb[beta:beta + 64, 16:32]
                             .rearrange("p (h one) -> p h one", one=1)
                             .broadcast_to([64, H, DKH]),
                             ALU.mult)
            vvt = sV.inc()

            if do_y:
                te.wait_ge(_sV, cb["banks"])
                for h in range(H):
                    hb = 64 * (h % 2)
                    te.matmul(pbank(h // 2, hb, hb + 64, 0, C),
                              vw[beta:beta + 64, tt * D + h * DKH:tt * D + (h + 1) * DKH],
                              ach_s[beta:beta + 64, h * C:(h + 1) * C],
                              start=True, stop=True)
                te.inc_last(_sT, 1)
                tyi = sT.inc()
                ve.wait_ge(_sT, tyi)
                ve.tensor_tensor(
                    ytw[:].rearrange("p (q t) -> p q t", q=8)[:, :, cc * C:(cc + 1) * C],
                    PS3[:, :, 0:C],
                    aqbc2.rearrange("p (q t) -> p q t", q=8)[:, :, (c % 2) * C:(c % 2) * C + C],
                    ALU.mult)
                cb["banks"] = sV.inc()
                te.wait_ge(_sV, cb["banks"])
                te.wait_ge(_sV, vsb)
                for h in range(H):
                    hb = 64 * (h % 2)
                    co = (h // 2) * W + cc * C
                    te.matmul(pbank(h // 2, hb, hb + 64, 0, C),
                              Sbb[hb:hb + 64, (h // 2) * DKH:(h // 2 + 1) * DKH],
                              qtw[hb:hb + 64, co:co + C],
                              start=True, stop=True)
                te.inc_last(_sT, 1)
                tyc = sT.inc()
                cb["sbb_rd"] = tyc
                ve.wait_ge(_sT, tyc)
                ve.tensor_tensor(mka_s[:, 0:512].rearrange("p (q t) -> p q t", q=8),
                                 PS3[:, :, 0:C],
                                 aqbc2.rearrange("p (q t) -> p q t", q=8)[:, :, (c % 2) * C:(c % 2) * C + C],
                                 ALU.mult)
                ve.tensor_tensor(
                    ytw[:].rearrange("p (q t) -> p q t", q=8)[:, :, cc * C:(cc + 1) * C],
                    ytw[:].rearrange("p (q t) -> p q t", q=8)[:, :, cc * C:(cc + 1) * C],
                    mka_s[:, 0:512].rearrange("p (q t) -> p q t", q=8),
                    ALU.add)
                cb["banks"] = sV.inc()

            ve.tensor_tensor(Ssb[:].rearrange("p (q j) -> p q j", q=8),
                             Ssb[:].rearrange("p (q j) -> p q j", q=8),
                             acols[:].rearrange("p (q c) -> p q c", q=8)[:, :, c:c + 1]
                             .broadcast_to([128, 8, DKH]),
                             ALU.mult)
            vsm = sV.inc()
            te.wait_ge(_sV, cb["banks"])
            te.wait_ge(_sV, vvt)
            for h in range(H):
                hb = 64 * (h % 2)
                te.matmul(pbank(h // 2, hb, hb + 64, 0, DKH),
                          ktokw[beta:beta + 64, tt * D + h * DKH:tt * D + (h + 1) * DKH],
                          vt_s[beta:beta + 64, h * DKH:(h + 1) * DKH],
                          start=True, stop=True)
            te.inc_last(_sT, 1)
            tp = sT.inc()
            ve.wait_ge(_sT, tp)
            ve.wait_ge(_sV, vsm)
            ve.tensor_tensor(Ssb[:].rearrange("p (q j) -> p q j", q=8),
                             Ssb[:].rearrange("p (q j) -> p q j", q=8),
                             PS3[:, :, 0:DKH], ALU.add)
            cb["banks"] = sV.inc()

        gates(XPB, self_extras=False, first=True)
        ve.memset(Ssb[:], 0.0)
        sV.inc()

        for win in range(NWIN):
            for bk in range(4):
                bank_busy[bk] = max(bank_busy.get(bk, 0), cb.get("banks", 0))
            window_proj_tok(win, XPB, WKT, KTOKW, phi_scale=1.0)
            window_proj_tok(win, XPB, WVT, VW, phi_scale=None)
            if bank_busy:
                cb["banks"] = max(cb.get("banks", 0), max(bank_busy.values()))
            for cc in range(CPW):
                chunk_P(win * CPW + cc, cc, KTOKW, VW, do_y=False)

        ve.wait_ge(_sX, X_CONST)
        ve.tensor_scalar(Ssb[:], Ssb[:], sel[:], None, ALU.mult)
        v_sin = sV.inc()

        te.wait_ge(_sV, v_sin)
        gates(XB, self_extras=True, first=False)
        cb.clear()
        bank_busy.clear()

        qtw, ktw, ytw = QTW(), KTW(), YTW()
        for win in range(NWIN):
            for bk in range(4):
                bank_busy[bk] = max(bank_busy.get(bk, 0), cb.get("banks", 0))
            window_proj_feat(win, XB, WQT, qtw, phi_scale=SCALE)
            window_proj_feat(win, XB, WKT, ktw, phi_scale=1.0)
            window_proj_tok(win, XB, WKT, KTOKW, phi_scale=1.0)
            window_proj_tok(win, XB, WVT, VW, phi_scale=None)
            if bank_busy:
                cb["banks"] = max(cb.get("banks", 0), max(bank_busy.values()))
            for cc in range(CPW):
                chunk_P(win * CPW + cc, cc, KTOKW, VW, do_y=True,
                        qtw=qtw, ktw=ktw, ytw=ytw)
            slabs = [ring_load(WOT[:, q * 256:(q + 1) * 256]
                               .rearrange("(k p) f -> p k f", p=128)) for q in range(4)]
            te.wait_ge(_sV, cb.get("banks", 0))
            for grp, o in enumerate(range(8)):
                slot, wld = slabs[o // 2]
                bk = grp % 4
                te.wait_ge(wld[0], wld[1])
                bank_wait_t(bk)
                psv = PSA[:, bk * 512:(bk + 1) * 512]
                for k in range(8):
                    te.matmul(psv,
                              slot[:, k * 256 + (o % 2) * 128:k * 256 + (o % 2) * 128 + 128],
                              ytw[:, k * W:(k + 1) * W],
                              start=(k == 0), stop=(k == 7))
                te.inc_last(_sT, 1)
                tmm = sT.inc()
                ve.wait_ge(_sT, tmm)
                rsl = R[:, o * T + win * W:o * T + (win + 1) * W]
                ve.tensor_tensor(rsl, rsl, psv, ALU.add)
                bank_busy[bk] = sV.inc()
            for _ in range(4):
                ring_done()

        stack.close()
        HT = AL("HT", [128, 16 * T], BF16)
        sg_s = AL("sg_s", [128, T], BF16)
        cwbc = AL("cwbc", [128, T], BF16)
        epsc = AL("epsc", [128, 1])
        m_sb = AL("m_sb", [128, 512])
        v1_sb = AL("v1_sb", [128, 512])
        rstd_sb = AL("rstd_sb", [128, 512])
        rq_s = AL("rq_s", [128, 512], BF16)
        prb_s = AL("prb_s", [128, NEXP])
        top_s = AL("top_s", [128, 2 * NEXP])
        m12_s = AL("m12_s", [128, 2])
        cwTb = AL("cwTb", [NEXP, T], BF16)
        cwrow = AL("cwrow", [1, T], BF16)

        def layernorm(cast_out):
            ve.wait_ge(_sT, sT.v)
            ve.memset(epsc[:], EPS)
            for k in range(8):
                ve.tensor_copy(XB[:, k * T:(k + 1) * T], R[:, k * T:(k + 1) * T])
            vcast = sV.inc()
            prev_apply_v = None
            for tc in range(4):
                s1 = PSA[:, 0:512]
                s2 = PSA[:, 512:1024]
                sq_marks = []
                for k in range(8):
                    sqv = rq_s[:]
                    sc.wait_ge(_sV, vcast)
                    if sq_marks:
                        sc.wait_ge(_sT, sq_marks[-1])
                    if tc > 0 and k == 0:
                        sc.wait_ge(_sV, prev_apply_v)
                    sc.activation(sqv, XB[:, k * T + tc * 512:k * T + (tc + 1) * 512],
                                  AF.Square)
                    ssq = sS.inc()
                    te.wait_ge(_sV, vcast)
                    te.wait_ge(_sS, ssq)
                    if tc > 0 and k == 0:
                        te.wait_ge(_sV, prev_apply_v)
                    te.matmul(s1, onesb[:], XB[:, k * T + tc * 512:k * T + (tc + 1) * 512],
                              start=(k == 0), stop=(k == 7))
                    te.matmul(s2, onesb[:], sqv, start=(k == 0), stop=(k == 7))
                    te.inc_last(_sT, 1)
                    sq_marks.append(sT.inc())
                ve.wait_ge(_sT, sq_marks[-1])
                ve.tensor_scalar(m_sb[:], s1, 1.0 / D, None, ALU.mult)
                ve.tensor_tensor(v1_sb[:], m_sb[:], m_sb[:], ALU.mult)
                ve.scalar_tensor_tensor(v1_sb[:], s2, 1.0 / D, v1_sb[:],
                                        ALU.mult, ALU.subtract)
                vvar = sV.inc()
                sc.wait_ge(_sV, vvar)
                sc.activation(v1_sb[:], v1_sb[:], AF.Sqrt, bias=epsc[:])
                ssm = sS.inc()
                ve.wait_ge(_sS, ssm)
                ve.reciprocal(rstd_sb[:], v1_sb[:])
                for k in range(8):
                    rsl = R[:, k * T + tc * 512:k * T + (tc + 1) * 512]
                    ve.tensor_tensor(rsl, rsl, m_sb[:], ALU.subtract)
                    ve.tensor_tensor(rsl, rsl, rstd_sb[:], ALU.mult)
                prev_apply_v = sV.inc()
            if cast_out:
                ve.wait_ge(_sV, prev_apply_v)
                for k in range(8):
                    ve.tensor_copy(XB[:, k * T:(k + 1) * T], R[:, k * T:(k + 1) * T])
                return sV.inc()
            return prev_apply_v

        v_x1b = layernorm(cast_out=True)

        tr_prev = {}
        for tt in range(16):
            pp = tt % 2
            psl = pbank(4 + 2 * pp, c1=NEXP)
            pst = pbank(5 + 2 * pp, c1=128)
            te.wait_ge(_sV, v_x1b)
            if pp in tr_prev:
                te.wait_ge(_sV, tr_prev[pp])
            for k in range(8):
                te.matmul(psl, R[:, k * T + tt * 128:k * T + (tt + 1) * 128],
                          wrtf[:, k * NEXP:(k + 1) * NEXP],
                          start=(k == 0), stop=False)
            te.matmul(psl, one1f[:], rbrow[:], start=False, stop=True)
            te.inc_last(_sT, 1)
            tl = sT.inc()
            sc.wait_ge(_sT, tl)
            if tr_prev:
                sc.wait_ge(_sV, max(tr_prev.values()))
            sc.activation(prb_s[:], psl, AF.Sigmoid)
            sprb = sS.inc()
            ve.wait_ge(_sS, sprb)
            ve.reduce_max(m12_s[:, 0:1], prb_s[:], axis=AXX)
            ve.tensor_scalar(top_s[:, 0:NEXP], prb_s[:], m12_s[:, 0:1], None, ALU.is_equal)
            ve.scalar_tensor_tensor(top_s[:, NEXP:2 * NEXP], top_s[:, 0:NEXP], -1e30,
                                    prb_s[:], ALU.mult, ALU.add)
            ve.reduce_max(m12_s[:, 1:2], top_s[:, NEXP:2 * NEXP], axis=AXX)
            ve.tensor_scalar(top_s[:, 0:NEXP], prb_s[:], m12_s[:, 1:2], None, ALU.is_ge)
            ve.tensor_tensor(prb_s[:], prb_s[:], top_s[:, 0:NEXP], ALU.mult)
            vcw = sV.inc()
            te.wait_ge(_sV, vcw)
            te.transpose(pst[0:NEXP, 0:128], prb_s[:], idf[:]).then_inc(_sT, 1)
            tct = sT.inc()
            ve.wait_ge(_sT, tct)
            ve.tensor_copy(cwTb[:, tt * 128:(tt + 1) * 128], pst[0:NEXP, 0:128])
            tr_prev[pp] = sV.inc()
        v_cwt = max(tr_prev.values())

        fb = {}

        def expert(e):
            wg_hbm = WGS if e == -1 else WGE[e]
            wu_hbm = WUS if e == -1 else WUE[e]
            wd_hbm = WDS if e == -1 else WDE[e]
            prod_marks = []
            for q in range(8):
                gsl = ring_load(wg_hbm[:, q * 256:(q + 1) * 256]
                                .rearrange("(k p) i -> p k i", p=128))
                usl = ring_load(wu_hbm[:, q * 256:(q + 1) * 256]
                                .rearrange("(k p) i -> p k i", p=128))
                for ii in range(2):
                    i_t = 2 * q + ii
                    te.wait_ge(gsl[1][0], gsl[1][1])
                    te.wait_ge(_sV, v_cwt)
                    if "d_bank" in fb:
                        te.wait_ge(_sV, max(fb["d_bank"].values()))
                        fb.pop("d_bank")
                    tg = {}
                    for tc in range(4):
                        if "g_bank" in fb:
                            te.wait_ge(_sS, fb["g_bank"][tc])
                        for k in range(8):
                            te.matmul(PSA[:, tc * 512:(tc + 1) * 512],
                                      gsl[0][:, k * 256 + ii * 128:k * 256 + ii * 128 + 128],
                                      XB[:, k * T + tc * 512:k * T + (tc + 1) * 512],
                                      start=(k == 0), stop=(k == 7))
                        te.inc_last(_sT, 1)
                        tg[tc] = sT.inc()
                    te.wait_ge(usl[1][0], usl[1][1])
                    tu = {}
                    for tc in range(4):
                        if "u_bank" in fb:
                            te.wait_ge(_sV, fb["u_bank"][tc])
                        for k in range(8):
                            te.matmul(PSB[:, tc * 512:(tc + 1) * 512],
                                      usl[0][:, k * 256 + ii * 128:k * 256 + ii * 128 + 128],
                                      XB[:, k * T + tc * 512:k * T + (tc + 1) * 512],
                                      start=(k == 0), stop=(k == 7))
                        te.inc_last(_sT, 1)
                        tu[tc] = sT.inc()
                    gb = {}
                    for tc in range(4):
                        sc.wait_ge(_sT, tg[tc])
                        if "sg_scr" in fb:
                            sc.wait_ge(_sV, fb["sg_scr"][tc])
                        sc.activation(sg_s[:, tc * 512:(tc + 1) * 512],
                                      PSA[:, tc * 512:(tc + 1) * 512], AF.Silu)
                        gb[tc] = sS.inc()
                    fb["g_bank"] = gb
                    ub = {}
                    for tc in range(4):
                        ve.wait_ge(_sT, tu[tc])
                        ve.wait_ge(_sS, gb[tc])
                        if "ht_rd" in fb:
                            ve.wait_ge(_sT, fb["ht_rd"])
                        if e >= 0:
                            ve.wait_ge(_sV, fb["cwbc_ok"])
                        ve.tensor_tensor(HT[:, i_t * T + tc * 512:i_t * T + (tc + 1) * 512],
                                         sg_s[:, tc * 512:(tc + 1) * 512],
                                         PSB[:, tc * 512:(tc + 1) * 512], ALU.mult)
                        if e >= 0:
                            ve.tensor_tensor(HT[:, i_t * T + tc * 512:i_t * T + (tc + 1) * 512],
                                             HT[:, i_t * T + tc * 512:i_t * T + (tc + 1) * 512],
                                             cwbc[:, tc * 512:(tc + 1) * 512], ALU.mult)
                        ub[tc] = sV.inc()
                    fb["u_bank"] = ub
                    fb["sg_scr"] = ub
                    prod_marks.append(ub[3])
                ring_done()
                ring_done()
            h_ready_v = prod_marks[-1]
            last_td = None
            for q in range(8):
                dsl = ring_load(wd_hbm[:, q * 128:(q + 1) * 128]
                                .rearrange("(k p) o -> p k o", p=128))
                for oo in range(1):
                    o_t = q
                    te.wait_ge(dsl[1][0], dsl[1][1])
                    te.wait_ge(_sV, h_ready_v)
                    if "g_bank" in fb:
                        te.wait_ge(_sS, max(fb["g_bank"].values()))
                        fb.pop("g_bank")
                    td = {}
                    for tc in range(4):
                        if "d_bank" in fb:
                            te.wait_ge(_sV, fb["d_bank"][tc])
                        for ki in range(16):
                            te.matmul(PSA[:, tc * 512:(tc + 1) * 512],
                                      dsl[0][:, ki * 128:ki * 128 + 128],
                                      HT[:, ki * T + tc * 512:ki * T + (tc + 1) * 512],
                                      start=(ki == 0), stop=(ki == 15))
                        te.inc_last(_sT, 1)
                        td[tc] = sT.inc()
                    last_td = td[3]
                    db = {}
                    for tc in range(4):
                        ve.wait_ge(_sT, td[tc])
                        rsl = R[:, o_t * T + tc * 512:o_t * T + (tc + 1) * 512]
                        ve.tensor_tensor(rsl, rsl, PSA[:, tc * 512:(tc + 1) * 512], ALU.add)
                        db[tc] = sV.inc()
                    fb["d_bank"] = db
                ring_done()
            fb["ht_rd"] = last_td
            ne = e + 1
            if ne <= NEXP - 1:
                gp.wait_ge(_sV, v_cwt)
                gp.wait_ge(_sW, sW.v)
                if "cwrow_rd" in fb:
                    gp.wait_ge(_sT, fb["cwrow_rd"])
                gp.dma_start(out=cwrow[:], in_=cwTb[ne:ne + 1, :]).then_inc(_sW, 16)
                wrow = sW.inc(16)
                te.wait_ge(_sW, wrow)
                if "u_bank" in fb:
                    te.wait_ge(_sV, max(fb["u_bank"].values()))
                for tc in range(4):
                    te.matmul(PSB[:, tc * 512:(tc + 1) * 512], one1b[:],
                              cwrow[0:1, tc * 512:(tc + 1) * 512],
                              start=True, stop=True)
                te.inc_last(_sT, 1)
                tcb = sT.inc()
                fb["cwrow_rd"] = tcb
                ve.wait_ge(_sT, tcb)
                for tc in range(4):
                    ve.tensor_copy(cwbc[:, tc * 512:(tc + 1) * 512],
                                   PSB[:, tc * 512:(tc + 1) * 512])
                m = sV.inc()
                fb["u_bank"] = {tc: m for tc in range(4)}
                fb["cwbc_ok"] = m

        for e in range(-1, NEXP):
            expert(e)

        layernorm(cast_out=False)
        v_done = sV.v

        sy.wait_ge(_sV, v_done)
        sy.dma_start(out=OUT[:].rearrange("(k p) t -> p k t", p=128),
                     in_=R[:].rearrange("p (k t) -> p k t", k=8)).then_inc(_sO, 16)
        sO.inc(16)
        sy.wait_ge(_sO, sO.v)

        @block.sync
        def _(eng):
            sy.replay(eng)

        @block.gpsimd
        def _(eng):
            gp.replay(eng)

        @block.tensor
        def _(eng):
            te.replay(eng)

        @block.vector
        def _(eng):
            ve.replay(eng)

        @block.scalar
        def _(eng):
            sc.replay(eng)

    return nc


_NC_CACHE = {}


def _get_nc():
    if "nc" not in _NC_CACHE:
        _NC_CACHE["nc"] = build_program()
    return _NC_CACHE["nc"]


def _consts():
    maskq = np.triu(np.ones((C, C), np.float32))
    MASKc = np.ascontiguousarray(np.concatenate([maskq, maskq], axis=0))
    IDFc = np.eye(128, dtype=np.float32)
    ONESBc = np.ones((128, 128), dtype=np.float32).astype(BFNP)
    ONE1Bc = np.ones((1, 128), dtype=np.float32).astype(BFNP)
    ONE1Fc = np.ones((1, 128), dtype=np.float32)
    selp = np.zeros((H, 8 * 128), np.float32)
    for p in range(8):
        selp[2 * p, p * 128:p * 128 + 64] = 1.0
        selp[2 * p + 1, p * 128 + 64:(p + 1) * 128] = 1.0
    return MASKc, IDFc, ONESBc, ONE1Bc, ONE1Fc, selp.astype(BFNP), selp


def kernel(L0_attn_x, L0_attn_Wq, L0_attn_Wk, L0_attn_Wv, L0_attn_Wo,
           L0_attn_Wg, L0_attn_bg, L0_ffn_Wg_s, L0_ffn_Wu_s, L0_ffn_Wd_s,
           L0_ffn_Wg, L0_ffn_Wu, L0_ffn_Wd, L0_ffn_Wr, L0_ffn_rb):
    x = np.ascontiguousarray(np.asarray(L0_attn_x, np.float32)[0])

    def tp(w):
        return np.ascontiguousarray(np.asarray(w, np.float32).T)

    MASKc, IDFc, ONESBc, ONE1Bc, ONE1Fc, SELPBc, SELPFc = _consts()
    common = dict(
        WQT=tp(L0_attn_Wq), WKT=tp(L0_attn_Wk), WVT=tp(L0_attn_Wv),
        WOT=tp(L0_attn_Wo), WGAT=tp(L0_attn_Wg),
        BG=np.ascontiguousarray(np.asarray(L0_attn_bg, np.float32)[:, None]),
        WRT=tp(L0_ffn_Wr),
        RB=np.ascontiguousarray(np.asarray(L0_ffn_rb, np.float32)[None, :]),
        WGS=tp(L0_ffn_Wg_s), WUS=tp(L0_ffn_Wu_s), WDS=tp(L0_ffn_Wd_s),
        WGE=np.ascontiguousarray(np.transpose(np.asarray(L0_ffn_Wg, np.float32), (0, 2, 1))),
        WUE=np.ascontiguousarray(np.transpose(np.asarray(L0_ffn_Wu, np.float32), (0, 2, 1))),
        WDE=np.ascontiguousarray(np.transpose(np.asarray(L0_ffn_Wd, np.float32), (0, 2, 1))),
        MASK=MASKc, IDF=IDFc, ONESB=ONESBc, ONE1B=ONE1Bc, ONE1F=ONE1Fc,
        SELPB=SELPBc, SELPF=SELPFc,
    )
    xts = [np.ascontiguousarray(x[j * T:(j + 1) * T].T) for j in range(8)]
    in_maps = []
    for j in range(8):
        m = dict(common)
        m["XT"] = xts[j]
        m["XPT"] = xts[j - 1] if j > 0 else xts[0]
        m["SEL"] = np.full((128, 1), 0.0 if j == 0 else 1.0, np.float32)
        in_maps.append(m)

    nc = _get_nc()
    res = run_bass_kernel_spmd(nc, in_maps, core_ids=list(range(8))).results
    out = np.concatenate([np.asarray(r["OUT"], np.float32).T for r in res], axis=0)
    return np.ascontiguousarray(out[None])
